# version 04 (final)
# baseline (speedup 1.0000x reference)
import numpy as np
import ml_dtypes
import concourse.bass as bass
import concourse.mybir as mybir
from concourse.bass_utils import run_bass_kernel_spmd
from contextlib import ExitStack

F32 = mybir.dt.float32
BF16 = mybir.dt.bfloat16
U8 = mybir.dt.uint8
AF = mybir.ActivationFunctionType
ALU = mybir.AluOpType
AX = mybir.AxisListType

B, S, D = 4, 2048, 2048
NH, HD = 16, 128
HID = 5632
HC = 44
DC = 16
TOK = 1024
NT = 8
SCALE_W = HD ** -0.5
C0, C1 = 4.0, 8.0
N_ITER = 30
RG = [[0, 1], [2, 3], [4, 5], [6, 7]]
bfloat16 = ml_dtypes.bfloat16


def _chunk_lhsT(wT):
    K, M = wT.shape
    return np.ascontiguousarray(
        wT.reshape(K // 128, 128, M // 128, 128).transpose(2, 1, 0, 3))


def _rope_tables():
    dim = HD
    inv_freq = 1.0 / 10000.0 ** (np.arange(0, dim, 2, dtype=np.float64) / dim)
    t = np.arange(S, dtype=np.float64)
    freqs = np.einsum("i,j->ij", t, inv_freq)
    freqs = np.concatenate((freqs, freqs), axis=-1)
    scale_base = (np.arange(0, dim, 2, dtype=np.float64) + 0.4 * dim) / (1.4 * dim)
    power = (t - S // 2) / 512.0
    xs = scale_base[None, :] ** power[:, None]
    xs = np.concatenate((xs, xs), axis=-1)
    sign = np.where(np.arange(dim) < dim // 2, -1.0, 1.0)[None, :]
    cq = np.cos(freqs) * xs * SCALE_W
    sq = np.sin(freqs) * xs * sign * SCALE_W
    ck = np.cos(freqs) / xs * SCALE_W
    sk = np.sin(freqs) / xs * sign * SCALE_W
    return np.ascontiguousarray(
        np.stack([cq.T, sq.T, ck.T, sk.T], 0).astype(np.float32))


def _host_prepare(inp):
    x = np.asarray(inp["L0_attn_hidden_states"], np.float32)

    def u8view(a):
        a = np.ascontiguousarray(np.asarray(a, np.int32))
        return a.view(np.uint8).reshape(*a.shape, 4)[..., 0]

    wq = np.asarray(inp["L0_attn_w_q"], np.float32)
    wkv = np.asarray(inp["L0_attn_w_kv"], np.float32)
    wproj = np.asarray(inp["L0_attn_w_proj"], np.float32)

    wqT = _chunk_lhsT(wq.T)
    wkT = _chunk_lhsT(np.ascontiguousarray(wkv[0:128].T))
    wvT = _chunk_lhsT(np.ascontiguousarray(wkv[128:256].T))
    g0 = u8view(inp["L0_ffn_wq_gate"])
    u0 = u8view(inp["L0_ffn_wq_up"])
    d0 = u8view(inp["L0_ffn_wq_down"])
    wg0 = _chunk_lhsT(np.ascontiguousarray(g0.T))
    wu0 = _chunk_lhsT(np.ascontiguousarray(u0.T))
    wd0 = np.ascontiguousarray(d0.T.reshape(44, 128, 16, 128).transpose(2, 1, 0, 3))
    l1q = _chunk_lhsT(np.asarray(inp["L1_attn_Wq"], np.float32))
    l1k = _chunk_lhsT(np.asarray(inp["L1_attn_Wk"], np.float32))
    l1v = _chunk_lhsT(np.asarray(inp["L1_attn_Wv"], np.float32))
    l1o = _chunk_lhsT(np.asarray(inp["L1_attn_Wo"], np.float32))
    g1 = np.asarray(inp["L1_ffn_gate_w"], np.float32)
    u1 = np.asarray(inp["L1_ffn_up_w"], np.float32)
    d1 = np.asarray(inp["L1_ffn_down_w"], np.float32)
    wg1 = _chunk_lhsT(np.ascontiguousarray(g1.T))
    wu1 = _chunk_lhsT(np.ascontiguousarray(u1.T))
    wd1 = np.ascontiguousarray(d1.T.reshape(44, 128, 16, 128).transpose(2, 1, 0, 3))

    tabs = _rope_tables()
    tri = np.zeros((4, 128, 512), np.float32)
    jj = np.arange(128)[:, None]
    ii = np.arange(512)[None, :]
    for r in range(4):
        tri[r] = (ii >= jj + 128 * r).astype(np.float32)

    def col128(v):
        return np.ascontiguousarray(np.asarray(v, np.float32).reshape(-1, 128).T)

    base = {
        "w_k": np.ascontiguousarray(wkT[0]),
        "w_v": np.ascontiguousarray(wvT[0]),
        "b_proj": np.asarray(inp["L0_attn_b_proj"], np.float32).reshape(1, 2048),
        "tabs": tabs, "tri": tri,
        "wg0": wg0, "wu0": wu0, "wd0": wd0,
        "sg0": col128(inp["L0_ffn_s_gate"]), "su0": col128(inp["L0_ffn_s_up"]),
        "sd0": col128(inp["L0_ffn_s_down"]),
        "l1q": l1q, "l1k": l1k, "l1v": l1v, "l1o": l1o,
        "bq1": col128(inp["L1_attn_bq"]), "bk1": col128(inp["L1_attn_bk"]),
        "bv1": col128(inp["L1_attn_bv"]), "bo1": col128(inp["L1_attn_bo"]),
        "wi_row": np.asarray(inp["L1_attn_Wi"], np.float32).reshape(1, 2048),
        "bi": np.ascontiguousarray(np.broadcast_to(
            np.asarray(inp["L1_attn_bi"], np.float32).reshape(1, 1), (128, 1))),
        "wg1": wg1, "wu1": wu1, "wd1": wd1,
        "sg1": col128(inp["L1_ffn_gate_s"]), "su1": col128(inp["L1_ffn_up_s"]),
        "sd1": col128(inp["L1_ffn_down_s"]),
        "ident": np.eye(128, dtype=np.float32),
        "sel8": np.ascontiguousarray(
            np.eye(8, dtype=np.float32)[:, :, None]
            * np.ones((1, 1, 128), np.float32)),
    }
    maps = []
    for c in range(8):
        b, h = c // 2, c % 2
        m = dict(base)
        m["x_full"] = np.ascontiguousarray(x[b])
        m["x_own"] = np.ascontiguousarray(x[b, h * 1024:(h + 1) * 1024])
        m["w_q_oct"] = np.ascontiguousarray(wqT[h * 8:(h + 1) * 8])
        m["w_o0"] = np.ascontiguousarray(
            wproj.T[h * 1024:(h + 1) * 1024].reshape(8, 128, 2048))
        maps.append(m)
    return maps


class Emit:
    ENGS = ("te", "dv", "sc", "sy", "gp")

    def __init__(self):
        self.q = {e: [] for e in self.ENGS}
        self.cnt = {}
        self.S = {}

    def mksem(self, nc, es, names):
        for n in names:
            self.S[n] = es.enter_context(nc.semaphore(n))
            self.cnt[n] = 0

    def op(self, eng, fn, sem=None, k=1):
        if sem is None:
            self.q[eng].append(fn)
            return None
        self.cnt[sem] += k
        s = self.S[sem]
        self.q[eng].append(lambda e, fn=fn, s=s, k=k: fn(e).then_inc(s, k))
        return self.cnt[sem]

    def wait(self, eng, sem, v):
        if v is None or v <= 0:
            return
        s = self.S[sem]
        self.q[eng].append(lambda e, s=s, v=v: e.wait_ge(s, v))

    def val(self, sem):
        return self.cnt[sem]


PHASES = 99
DEBUG_DUMP = False


def build_nc():
    nc = bass.Bass("TRN2", target_bir_lowering=False)
    dram = lambda n, s, d, k="ExternalInput": nc.dram_tensor(n, s, d, kind=k).ap()

    x_full = dram("x_full", [S, D], F32)
    x_own = dram("x_own", [TOK, D], F32)
    w_q_oct = dram("w_q_oct", [8, 128, 16, 128], F32)
    w_k_d = dram("w_k", [128, 16, 128], F32)
    w_v_d = dram("w_v", [128, 16, 128], F32)
    w_o0_d = dram("w_o0", [8, 128, 2048], F32)
    b_proj_d = dram("b_proj", [1, 2048], F32)
    tabs_d = dram("tabs", [4, 128, S], F32)
    tri_d = dram("tri", [4, 128, 512], F32)
    wg0_d = dram("wg0", [44, 128, 16, 128], U8)
    wu0_d = dram("wu0", [44, 128, 16, 128], U8)
    wd0_d = dram("wd0", [16, 128, 44, 128], U8)
    sg0_d = dram("sg0", [128, 44], F32)
    su0_d = dram("su0", [128, 44], F32)
    sd0_d = dram("sd0", [128, 16], F32)
    l1q_d = dram("l1q", [16, 128, 16, 128], F32)
    l1k_d = dram("l1k", [16, 128, 16, 128], F32)
    l1v_d = dram("l1v", [16, 128, 16, 128], F32)
    l1o_d = dram("l1o", [16, 128, 16, 128], F32)
    bq1_d = dram("bq1", [128, 16], F32)
    bk1_d = dram("bk1", [128, 16], F32)
    bv1_d = dram("bv1", [128, 16], F32)
    bo1_d = dram("bo1", [128, 16], F32)
    wi_d = dram("wi_row", [1, 2048], F32)
    bi_d = dram("bi", [128, 1], F32)
    wg1_d = dram("wg1", [44, 128, 16, 128], F32)
    wu1_d = dram("wu1", [44, 128, 16, 128], F32)
    wd1_d = dram("wd1", [16, 128, 44, 128], F32)
    sg1_d = dram("sg1", [128, 44], F32)
    su1_d = dram("su1", [128, 44], F32)
    sd1_d = dram("sd1", [128, 16], F32)
    ident_d = dram("ident", [128, 128], F32)
    sel8_d = dram("sel8", [8, 8, 128], F32)
    out_d = dram("out_own", [TOK, D], F32, "ExternalOutput")
    dbg_d = dram("dbg_ar", [128, 34816], BF16, "ExternalOutput") if DEBUG_DUMP else None
    dbg2_d = dram("dbg2", [128, 33792], BF16, "ExternalOutput") if DEBUG_DUMP == 2 else None
    dbg3_d = dram("dbg3", [128, 32], F32, "ExternalOutput") if DEBUG_DUMP == 2 else None

    rs0_in = dram("rs0_in", [S, D], BF16, "Internal")
    rs0_out = dram("rs0_out", [TOK, D], BF16, "Internal")
    agi_in = dram("agi_in", [8, 128], F32, "Internal")
    agi_out = dram("agi_out", [16, 128], F32, "Internal")
    agkv_in = dram("agkv_in", [4096, 1024], BF16, "Internal")
    agkv_out = dram("agkv_out", [8192, 1024], BF16, "Internal")

    em = Emit()
    es = ExitStack()

    def sb(name, shape, dt):
        return es.enter_context(nc.sbuf_tensor(name, shape, dt))

    with es:
        resid = sb("resid", [128, NT, 2048], F32)
        ar_addr = nc.sbuf_base
        AR = sb("AR", [128, 45056], BF16)
        xfm = sb("xfm", [128, 16, TOK], BF16)
        wsl = sb("wsl", [128, 3, 2048], BF16)
        stage = sb("stage", [128, 2, 512], BF16)
        stageT = sb("stageT", [128, 2, 4, 128], BF16)
        gsil = sb("gsil", [128, 2, 512], BF16)
        ones_bf = sb("ones_bf", [128, 128], BF16)
        ones_f = sb("ones_f", [1, 128], F32)
        identb = sb("identb", [128, 128], F32)
        impT = sb("impT", [8, 128], F32)
        sgc0 = sb("sgc0", [128, 44], F32)
        suc0 = sb("suc0", [128, 44], F32)
        sdc0 = sb("sdc0", [128, 16], F32)
        sgc1 = sb("sgc1", [128, 44], F32)
        suc1 = sb("suc1", [128, 44], F32)
        sdc1 = sb("sdc1", [128, 16], F32)
        bqc = sb("bqc", [128, 16], F32)
        bkc = sb("bkc", [128, 16], F32)
        bvc = sb("bvc", [128, 16], F32)
        boc = sb("boc", [128, 16], F32)
        bic = sb("bic", [128, 1], F32)
        s16 = sb("s16", [128, 16], F32)
        s16b = sb("s16b", [128, 16], F32)
        impown = sb("impown", [128, 8], F32)
        col = {n: sb(n, [128, 1], F32) for n in
               ["mean0", "mean1", "rstd0", "rstd1", "nmr0", "nmr1",
                "thr_c", "mid_c", "sca_c", "scb_c", "scc_c", "bcol_c",
                "c_n4", "c_n8", "c_eps"]}
        P = [es.enter_context(nc.psum_tensor(f"p{i}", [128, 512], F32))
             for i in range(8)]

        def ov(name, shape, dt, elem_off):
            return nc.alloc_sbuf_tensor_at(
                name, shape, dt, offset=ar_addr + elem_off * 2)

        TABS3 = ov("tabs_ov", [128, 4, 512], F32, 24576)
        RSCRa = ov("rscra", [128, 512], F32, 28672)
        RSCR2a = ov("rscr2a", [128, 512], F32, 29696)
        XBFa = ov("xbfa", [128, 2048], BF16, 43008)
        BIASF = ov("bias_ov", [128, 2048], F32, 36864)
        IMPB = ov("imp_ov", [128, 2048], F32, 0)
        XBFb = ov("xbfb", [128, 2, 2048], BF16, 16384)
        RSCRb = ov("rscrb", [128, 512], F32, 38912)
        RSCRc = ov("rscrc", [128, 512], F32, 40960)
        SEL8 = ov("sel8_ov", [8, 1024], F32, 8192)

        em.mksem(nc, es, [
            "pe", "dv", "sc", "gp", "ld",
            "wl0", "wl1", "wl2", "xb0", "xb1", "xt0", "xt1", "tb", "vt",
            "kh0", "kh1", "sg0", "sg1", "rb0", "rb1",
            "xf0", "xf1", "xf2", "xf3", "xf4", "xf5", "xf6", "xf7",
            "imps", "cc", "st", "mis", "msy",
        ])
        em.cnt["xt1"] = 0

        def arv(off, shape):
            n = int(np.prod(shape))
            a = AR[:, off:off + n]
            if len(shape) == 1:
                return a
            if len(shape) == 2:
                return a.rearrange("p (a b) -> p a b", a=shape[0])
            return a.rearrange("p (a b c) -> p a b c", a=shape[0], b=shape[1])

        te = lambda fn: em.op("te", fn, "pe")

        def dv(fn):
            em.wait("dv", "dv", em.val("dv"))
            return em.op("dv", fn, "dv")

        def sc(fn):
            em.wait("sc", "sc", em.val("sc"))
            return em.op("sc", fn, "sc")

        def gp(fn):
            em.wait("gp", "gp", em.val("gp"))
            return em.op("gp", fn, "gp")

        QROT = arv(0, (8, 2048))
        OUT0 = arv(16384, (8, 2048))
        XW = [arv(16384, (16, 512))]
        KROT = arv(32768, (2048,))
        VTM = arv(34816, (16, 128))
        VFM = arv(36864, (2048,))
        TRI = arv(36864, (4, 512))
        WK = arv(38912, (16, 128))
        WV = arv(40960, (16, 128))
        EB = arv(43008, (4, 512))
        WO0 = arv(0, (8, 2048))
        RB = [arv(32768, (2048,)), arv(34816, (2048,))]
        BROW = AR[0:1, 40960:40960 + 2048]
        HACT = arv(0, (44, 1024))
        GTB = arv(8192, (2048,))
        MASK = arv(16384, (1024,))
        QFM = arv(0, (16, 1024))
        OUT1 = arv(16384, (16, 1024))
        KH = [arv(32768, (2048,)), arv(34816, (2048,))]
        VFMh = arv(36864, (2048,))
        VTMh = arv(38912, (16, 128))

        for t in range(NT):
            em.op("sy", lambda s, t=t: s.dma_start(
                out=resid[:, t, :], in_=x_own[t * 128:(t + 1) * 128, :]), "ld", 16)
        for dst, src in [(sgc0, sg0_d), (suc0, su0_d), (sdc0, sd0_d),
                         (sgc1, sg1_d), (suc1, su1_d), (sdc1, sd1_d),
                         (bqc, bq1_d), (bkc, bk1_d), (bvc, bv1_d), (boc, bo1_d),
                         (bic, bi_d), (identb, ident_d)]:
            em.op("sy", lambda s, d=dst, r=src: s.dma_start(out=d[:, :], in_=r[:, :]), "ld", 16)
        ld_total = em.val("ld")
        gp(lambda g: g.memset(ones_bf[:], 1.0))
        gp(lambda g: g.memset(ones_f[:], 1.0))
        gp(lambda g: g.memset(col["c_n4"][:, :], -C0))
        gp(lambda g: g.memset(col["c_n8"][:, :], -C1))
        gp(lambda g: g.memset(col["c_eps"][:, :], 1e-5))
        gp_init = em.val("gp")
        em.op("gp", lambda g: g.dma_start(out=WK[:, :, :], in_=w_k_d[:, :, :]), "mis", 16)
        em.op("gp", lambda g: g.dma_start(out=WV[:, :, :], in_=w_v_d[:, :, :]), "mis", 16)
        mis_kv = em.val("mis")


        def _finish():
            em.wait("sy", "dv", em.val("dv"))
            em.wait("sy", "sc", em.val("sc"))
            em.wait("sy", "pe", em.val("pe"))
            if DEBUG_DUMP == 1:
                em.op("sy", lambda s: s.dma_start(
                    out=dbg_d[:, 0:16384], in_=QROT[:, :, :].rearrange("p a b -> p (a b)")), "st", 16)
                em.op("sy", lambda s: s.dma_start(
                    out=dbg_d[:, 16384:32768], in_=OUT0[:, :, :].rearrange("p a b -> p (a b)")), "st", 16)
                em.op("sy", lambda s: s.dma_start(
                    out=dbg_d[:, 32768:34816], in_=KROT[:, :]), "st", 16)

            for t in range(NT):
                em.op("sy", lambda s, t=t: s.dma_start(
                    out=out_d[t * 128:(t + 1) * 128, :], in_=resid[:, t, :]), "st", 16)
            em.wait("sy", "st", em.val("st"))
            _replay()

        def _replay():
            block = es.enter_context(nc.Block())

            @block.tensor
            def _(e):
                for f in em.q["te"]:
                    f(e)

            @block.vector
            def _(e):
                for f in em.q["dv"]:
                    f(e)

            @block.scalar
            def _(e):
                for f in em.q["sc"]:
                    f(e)

            @block.sync
            def _(e):
                for f in em.q["sy"]:
                    f(e)

            @block.gpsimd
            def _(e):
                for f in em.q["gp"]:
                    f(e)

        if PHASES <= 0:
            _finish()
            return nc
        pe_slot = {i: em.val("pe") for i in range(3)}
        pe_xw = {0: None}
        dv_tabs = None
        dv_psum_q = {0: None, 1: None}
        dv_psum_kv = {2: None, 3: None}

        def rotary(pp, tab_c, tab_s, dst):
            dv(lambda v: v.tensor_tensor(RSCRa[0:64, :], pp[64:128, :], tab_s[0:64, :], ALU.mult))
            dv(lambda v: v.tensor_tensor(RSCRa[64:128, :], pp[0:64, :], tab_s[64:128, :], ALU.mult))
            dv(lambda v: v.tensor_tensor(RSCR2a[:, :], pp[:, :], tab_c[:, :], ALU.mult))
            dv(lambda v: v.tensor_tensor(dst, RSCR2a[:, :], RSCRa[:, :], ALU.add))

        for w in range(4):
            slot_w = 0
            for t in range(4):
                em.wait("gp", "xt0", em.val("xt0"))
                em.op("gp", lambda g, w=w, t=t: g.dma_start(
                    out=XBFa[:, :],
                    in_=x_full[w * 512 + t * 128: w * 512 + (t + 1) * 128, :]),
                    "xb0", 16)
                em.wait("sy", "xb0", em.val("xb0"))
                if t == 0:
                    em.wait("sy", "pe", pe_xw[0])
                em.op("sy", lambda s, t=t: s.dma_start_transpose(
                    XW[0][:, :, t * 128:(t + 1) * 128], XBFa[:, :]), "xt0", 16)
            em.wait("sy", "dv", dv_tabs)
            for i in range(4):
                em.op("sy", lambda s, i=i, w=w: s.dma_start(
                    out=TABS3[:, i, :], in_=tabs_d[i, :, w * 512:(w + 1) * 512]), "tb", 16)
            tb_ready = em.val("tb")
            em.wait("te", "xt0", em.val("xt0"))
            em.wait("te", "xt1", em.val("xt1"))
            if w == 0:
                em.wait("te", "mis", mis_kv)
                em.wait("te", "gp", gp_init)
                em.wait("dv", "ld", ld_total)
            for hh in range(8):
                slot = hh % 3
                em.wait("gp", "pe", pe_slot[slot])
                em.op("gp", lambda g, hh=hh, slot=slot: g.dma_start(
                    out=wsl[:, slot, :].rearrange("p (a b) -> p a b", a=16),
                    in_=w_q_oct[hh, :, :, :]),
                    f"wl{slot}", 16)
                em.wait("te", f"wl{slot}", em.val(f"wl{slot}"))
                pp = P[hh % 2]
                em.wait("te", "dv", dv_psum_q[hh % 2])
                wv_ = wsl[:, slot, :].rearrange("p (a b) -> p a b", a=16)
                for dc in range(16):
                    te(lambda t_, pp=pp, wv_=wv_, dc=dc, sw=slot_w: t_.matmul(
                        pp[:, :], wv_[:, dc, :], XW[sw][:, dc, :],
                        start=(dc == 0), stop=(dc == 15)))
                pe_slot[slot] = em.val("pe")
                em.wait("dv", "pe", em.val("pe"))
                em.wait("dv", "tb", tb_ready)
                rotary(pp, TABS3[:, 0, :], TABS3[:, 1, :],
                       QROT[:, hh, w * 512:(w + 1) * 512])
                dv_psum_q[hh % 2] = em.val("dv")
            for which, pidx in (("k", 2), ("v", 3)):
                em.wait("te", "dv", dv_psum_kv[pidx])
                Wt = WK if which == "k" else WV
                pp = P[pidx]
                for dc in range(16):
                    te(lambda t_, pp=pp, Wt=Wt, dc=dc, sw=slot_w: t_.matmul(
                        pp[:, :], Wt[:, dc, :], XW[sw][:, dc, :],
                        start=(dc == 0), stop=(dc == 15)))
                em.wait("dv", "pe", em.val("pe"))
                if which == "k":
                    rotary(pp, TABS3[:, 2, :], TABS3[:, 3, :],
                           KROT[:, w * 512:(w + 1) * 512])
                else:
                    dv(lambda v, pp=pp, w=w: v.tensor_copy(
                        VFM[:, w * 512:(w + 1) * 512], pp[:, :]))
                dv_psum_kv[pidx] = em.val("dv")
            dv_tabs = em.val("dv")
            pe_xw[0] = em.val("pe")
            em.wait("sy", "dv", dv_psum_kv[3])
            em.op("sy", lambda s, w=w: s.dma_start_transpose(
                VTM[:, 4 * w:4 * (w + 1), :], VFM[:, w * 512:(w + 1) * 512]), "vt", 16)
        vt_all = em.val("vt")
        pe_l0a1_done = em.val("pe")
        em.wait("gp", "vt", vt_all)
        for r in range(4):
            em.op("gp", lambda g, r=r: g.dma_start(
                out=TRI[:, r, :], in_=tri_d[r, :, :]), "mis", 16)
        mis_tri = em.val("mis")

        if PHASES <= 1:
            _finish()
            return nc
        em.wait("te", "vt", vt_all)
        em.wait("te", "dv", em.val("dv"))
        em.wait("sc", "gp", gp_init)
        sc_exp_psum = {0: em.val("sc"), 1: em.val("sc")}
        pe_slotE = [em.val("pe")] * 4
        dv_out_evac = em.val("dv")
        ecnt = 0
        for hh in range(8):
            for qt in range(4):
                jmax = 4 * (qt + 1)
                for jc in range(jmax):
                    sp = jc % 2
                    em.wait("te", "sc", sc_exp_psum[sp])
                    te(lambda t_, hh=hh, qt=qt, jc=jc, sp=sp: t_.matmul(
                        P[sp][:, :], KROT.rearrange("p (a b) -> p a b", a=16)[:, jc, :],
                        QROT[:, hh, qt * 512:(qt + 1) * 512],
                        start=True, stop=True))
                    pe_sc = em.val("pe")
                    esl = ecnt % 4
                    ecnt += 1
                    em.wait("sc", "pe", pe_sc)
                    em.wait("sc", "pe", pe_slotE[esl])
                    sc(lambda s_, sp=sp, esl=esl: s_.activation(
                        out=EB[:, esl, :], in_=P[sp][:, :], func=AF.Exp,
                        bias=col["c_n4"][:, :], scale=1.0))
                    sc_exp_psum[sp] = em.val("sc")
                    r = jc - 4 * qt
                    if r >= 0:
                        em.wait("dv", "mis", mis_tri)
                        em.wait("dv", "sc", em.val("sc"))
                        dv(lambda v, esl=esl, r=r: v.tensor_tensor(
                            EB[:, esl, :], EB[:, esl, :], TRI[:, r, :], ALU.mult))
                        em.wait("te", "dv", em.val("dv"))
                    else:
                        em.wait("te", "sc", em.val("sc"))
                    if jc == 0:
                        em.wait("te", "dv", dv_out_evac)
                    te(lambda t_, esl=esl, jc=jc, jmax=jmax: t_.matmul(
                        P[4][:, :], ones_bf[:, :], EB[:, esl, :],
                        start=(jc == 0), stop=(jc == jmax - 1)))
                    te(lambda t_, esl=esl, jc=jc, jmax=jmax: t_.matmul(
                        P[5][:, :], VTM[:, jc, :], EB[:, esl, :],
                        start=(jc == 0), stop=(jc == jmax - 1)))
                    pe_slotE[esl] = em.val("pe")
                em.wait("dv", "pe", em.val("pe"))
                dv(lambda v: v.reciprocal(RSCRb[:, :], P[4][:, :]))
                dv(lambda v, hh=hh, qt=qt: v.tensor_tensor(
                    OUT0[:, hh, qt * 512:(qt + 1) * 512], P[5][:, :],
                    RSCRb[:, :], ALU.mult))
                dv_out_evac = em.val("dv")
        pe_l0a2_done = em.val("pe")
        dv_l0a2_done = em.val("dv")

        if PHASES <= 2:
            _finish()
            return nc
        em.wait("gp", "pe", pe_l0a2_done)
        for hc in range(8):
            em.op("gp", lambda g, hc=hc: g.dma_start(
                out=WO0[:, hc, :], in_=w_o0_d[hc, :, :]), "mis", 16)
        mis_wo = em.val("mis")
        em.wait("te", "mis", mis_wo)
        em.wait("te", "dv", dv_l0a2_done)
        sg_done = {0: em.val("sg0"), 1: em.val("sg1")}
        dv_evac_grp = {0: em.val("dv"), 1: em.val("dv")}
        for tc in range(16):
            grp = tc % 2
            em.wait("te", "dv", dv_evac_grp[grp])
            for hc in range(8):
                for dt in range(4):
                    te(lambda t_, tc=tc, hc=hc, dt=dt, grp=grp: t_.matmul(
                        P[4 * grp + dt][:, :],
                        OUT0[:, hc, tc * 128:(tc + 1) * 128],
                        WO0[:, hc, dt * 512:(dt + 1) * 512],
                        start=(hc == 0), stop=(hc == 7)))
            pe_grp = em.val("pe")
            em.wait("dv", "pe", pe_grp)
            for dt in range(4):
                ss = (tc * 4 + dt) % 2
                em.wait("dv", f"sg{ss}", sg_done[ss])
                dv(lambda v, dt=dt, grp=grp, ss=ss: v.tensor_copy(
                    stage[:, ss, :], P[4 * grp + dt][:, :]))
                em.wait("sy", "dv", em.val("dv"))
                em.op("sy", lambda s, tc=tc, dt=dt, ss=ss: s.dma_start(
                    out=rs0_in[tc * 128:(tc + 1) * 128, dt * 512:(dt + 1) * 512],
                    in_=stage[:, ss, :]), f"sg{ss}", 16)
                sg_done[ss] = em.val(f"sg{ss}")
            dv_evac_grp[grp] = em.val("dv")
        em.wait("gp", "sg0", em.val("sg0"))
        em.wait("gp", "sg1", em.val("sg1"))
        em.op("gp", lambda g: g.collective_compute(
            "ReduceScatter", ALU.add, replica_groups=RG,
            ins=[rs0_in[:, :].opt()], outs=[rs0_out[:, :].opt()]), "cc", 1)
        cc_rs0 = em.val("cc")
        em.op("gp", lambda g: g.dma_start(out=BROW[:, :], in_=b_proj_d[:, :]), "mis", 16)
        em.wait("te", "mis", em.val("mis"))
        em.wait("te", "dv", em.val("dv"))
        for dt in range(4):
            te(lambda t_, dt=dt: t_.matmul(
                P[dt][:, :], ones_bf[0:1, :], BROW[:, dt * 512:(dt + 1) * 512],
                start=True, stop=True))
        em.wait("dv", "pe", em.val("pe"))
        for dt in range(4):
            dv(lambda v, dt=dt: v.tensor_copy(
                BIASF[:, dt * 512:(dt + 1) * 512], P[dt][:, :]))
        dv_bias = em.val("dv")

        def layer_norm(t, make_xfm=True):
            s_ = t % 2
            ssq = s16[:, 8 + s_:9 + s_]
            sx = s16[:, 10 + s_:11 + s_]
            var = s16[:, 12 + s_:13 + s_]
            m2 = s16[:, 14 + s_:15 + s_]
            em.wait("sc", "dv", em.val("dv"))
            em.wait("sc", f"xf{(t + 6) % 8}", em.val(f"xf{(t + 6) % 8}"))
            sc(lambda s2, t=t, s_=s_, ssq=ssq: s2.activation(
                out=XBFb[:, s_, :], in_=resid[:, t, :], func=AF.Square,
                bias=0.0, scale=1.0, accum_out=ssq))
            em.wait("dv", "sc", em.val("sc"))
            dv(lambda v, t=t, sx=sx: v.reduce_sum(sx, resid[:, t, :], axis=AX.X))
            mc, rc, nc_ = col[f"mean{s_}"], col[f"rstd{s_}"], col[f"nmr{s_}"]
            dv(lambda v, sx=sx, mc=mc: v.tensor_scalar(mc[:, :], sx, 1.0 / 2048.0, None, ALU.mult))
            dv(lambda v, mc=mc, m2=m2: v.tensor_tensor(m2, mc[:, :], mc[:, :], ALU.mult))
            dv(lambda v, ssq=ssq, var=var: v.tensor_scalar(var, ssq, 1.0 / 2048.0, None, ALU.mult))
            dv(lambda v, var=var, m2=m2: v.tensor_tensor(var, var, m2, ALU.subtract))
            em.wait("sc", "dv", em.val("dv"))
            sc(lambda s2, var=var, s_=s_: s2.activation(
                out=s16b[:, 8 + s_:9 + s_], in_=var, func=AF.Sqrt,
                bias=col["c_eps"][:, :], scale=1.0))
            em.wait("dv", "sc", em.val("sc"))
            dv(lambda v, rc=rc, s_=s_: v.reciprocal(rc[:, :], s16b[:, 8 + s_:9 + s_]))
            dv(lambda v, mc=mc, rc=rc, nc_=nc_: v.tensor_scalar(
                nc_[:, :], mc[:, :], rc[:, :], -1.0, ALU.mult, ALU.mult))
            if make_xfm:
                dv(lambda v, t=t, s_=s_, mc=mc, rc=rc: v.tensor_scalar(
                    XBFb[:, s_, :], resid[:, t, :], mc[:, :], rc[:, :],
                    ALU.subtract, ALU.mult))
            dv(lambda v, t=t, mc=mc, rc=rc: v.tensor_scalar(
                resid[:, t, :], resid[:, t, :], mc[:, :], rc[:, :],
                ALU.subtract, ALU.mult))
            if make_xfm:
                em.wait("sy", "dv", em.val("dv"))
                em.op("sy", lambda s2, t=t, s_=s_: s2.dma_start_transpose(
                    xfm[:, :, t * 128:(t + 1) * 128], XBFb[:, s_, :]), f"xf{t}", 16)

        rb_done = {0: None, 1: None}
        for t in range(NT):
            r = t % 2
            em.wait("sy", "cc", cc_rs0)
            em.wait("sy", "dv", rb_done[r])
            em.op("sy", lambda s, t=t, r=r: s.dma_start(
                out=RB[r][:, :], in_=rs0_out[t * 128:(t + 1) * 128, :]), f"rb{r}", 16)
            em.wait("dv", f"rb{r}", em.val(f"rb{r}"))
            dv(lambda v, t=t, r=r: v.tensor_tensor(
                resid[:, t, :], resid[:, t, :], RB[r][:, :], ALU.add))
            rb_done[r] = em.val("dv")
            dv(lambda v, t=t: v.tensor_tensor(
                resid[:, t, :], resid[:, t, :], BIASF[:, :], ALU.add))
            layer_norm(t, make_xfm=True)

        if PHASES <= 3:
            _finish()
            return nc
        def ffn(layer):
            wg_d = wg0_d if layer == 0 else wg1_d
            wu_d = wu0_d if layer == 0 else wu1_d
            wd_d = wd0_d if layer == 0 else wd1_d
            sg_c = sgc0 if layer == 0 else sgc1
            su_c = suc0 if layer == 0 else suc1
            sd_c = sdc0 if layer == 0 else sdc1
            for t in range(NT):
                em.wait("te", f"xf{t}", em.val(f"xf{t}"))
            pe_wslot = {i: em.val("pe") for i in range(3)}
            evac_done = {0: em.val("dv"), 1: em.val("dv")}
            sc_evac_done = {0: em.val("sc"), 1: em.val("sc")}
            dv_gsil = {0: em.val("dv"), 1: em.val("dv")}
            nsl = 0
            for hc in range(44):
                grp = hc % 2
                slots = []
                for which, wdram in (("g", wg_d), ("u", wu_d)):
                    slot = nsl % 3
                    nsl += 1
                    em.wait("gp", "pe", pe_wslot[slot])
                    em.op("gp", lambda g, hc=hc, slot=slot, wdram=wdram: g.dma_start(
                        out=wsl[:, slot, :].rearrange("p (a b) -> p a b", a=16),
                        in_=wdram[hc, :, :, :]),
                        f"wl{slot}", 16)
                    wlv = em.val(f"wl{slot}")
                    if layer == 0:
                        em.wait("dv", f"wl{slot}", wlv)
                        dv(lambda v, slot=slot: v.tensor_scalar(
                            wsl[:, slot, :], wsl[:, slot, :], -128.0, None, ALU.add))
                        em.wait("te", "dv", em.val("dv"))
                    else:
                        em.wait("te", f"wl{slot}", wlv)
                    slots.append(slot)
                em.wait("te", "dv", evac_done[grp])
                em.wait("te", "sc", sc_evac_done[grp])
                for which, slot, poff in (("g", slots[0], 0), ("u", slots[1], 2)):
                    wvv = wsl[:, slot, :].rearrange("p (a b) -> p a b", a=16)
                    for tt in range(2):
                        pp = P[4 * grp + poff + tt]
                        for dc in range(16):
                            te(lambda t_, pp=pp, wvv=wvv, dc=dc, tt=tt: t_.matmul(
                                pp[:, :], wvv[:, dc, :],
                                xfm[:, dc, tt * 512:(tt + 1) * 512],
                                start=(dc == 0), stop=(dc == 15)))
                    pe_wslot[slot] = em.val("pe")
                pe_grp = em.val("pe")
                for tt in range(2):
                    em.wait("sc", "pe", pe_grp)
                    em.wait("sc", "dv", dv_gsil[tt])
                    sc(lambda s_, hc=hc, sg_c=sg_c: s_.copy(
                        col["sca_c"][:, :], sg_c[:, hc:hc + 1]))
                    sc(lambda s_, grp=grp, tt=tt: s_.activation(
                        out=gsil[:, tt, :], in_=P[4 * grp + tt][:, :], func=AF.Silu,
                        bias=0.0, scale=col["sca_c"][:, :]))
                    em.wait("dv", "sc", em.val("sc"))
                    dv(lambda v, hc=hc, su_c=su_c: v.tensor_copy(
                        col["scb_c"][:, :], su_c[:, hc:hc + 1]))
                    dv(lambda v, hc=hc, grp=grp, tt=tt: v.scalar_tensor_tensor(
                        HACT[:, hc, tt * 512:(tt + 1) * 512],
                        P[4 * grp + 2 + tt][:, :], col["scb_c"][:, :],
                        gsil[:, tt, :], ALU.mult, ALU.mult))
                    dv_gsil[tt] = em.val("dv")
                evac_done[grp] = em.val("dv")
                sc_evac_done[grp] = em.val("sc")
            segs = [(0, 15), (15, 30), (30, 44)]
            pe_wslot = {i: em.val("pe") for i in range(3)}
            dvstage = {0: em.val("sg0"), 1: em.val("sg1")}
            dv_add_done = {0: em.val("dv"), 1: em.val("dv")}
            nsl = 0
            for dcout in range(16):
                grp = dcout % 2
                em.wait("te", "dv", dv_add_done[grp])
                seg_slots = []
                for (a, b) in segs:
                    slot = nsl % 3
                    nsl += 1
                    em.wait("gp", "pe", pe_wslot[slot])
                    em.op("gp", lambda g, dcout=dcout, a=a, b=b, slot=slot, wd_d=wd_d: g.dma_start(
                        out=wsl[:, slot, 0:(b - a) * 128].rearrange(
                            "p (a b) -> p a b", b=128),
                        in_=wd_d[dcout, :, a:b, :]),
                        f"wl{slot}", 16)
                    wlv = em.val(f"wl{slot}")
                    if layer == 0:
                        em.wait("dv", f"wl{slot}", wlv)
                        dv(lambda v, slot=slot, n=(b - a) * 128: v.tensor_scalar(
                            wsl[:, slot, 0:n], wsl[:, slot, 0:n], -128.0, None, ALU.add))
                        seg_slots.append((slot, a, b, None, em.val("dv")))
                    else:
                        seg_slots.append((slot, a, b, wlv, None))
                for tt in range(2):
                    pp = P[2 * grp + tt]
                    for (slot, a, b, wlv, dvv) in seg_slots:
                        if tt == 0:
                            if wlv is not None:
                                em.wait("te", f"wl{slot}", wlv)
                            if dvv is not None:
                                em.wait("te", "dv", dvv)
                        wvv = wsl[:, slot, 0:(b - a) * 128].rearrange(
                            "p (a b) -> p a b", b=128)
                        for i, hc in enumerate(range(a, b)):
                            te(lambda t_, pp=pp, wvv=wvv, i=i, hc=hc, tt=tt: t_.matmul(
                                pp[:, :], wvv[:, i, :],
                                HACT[:, hc, tt * 512:(tt + 1) * 512],
                                start=(hc == 0), stop=(hc == 43)))
                    for (slot, a, b, wlv, dvv) in seg_slots:
                        if tt == 1:
                            pe_wslot[slot] = em.val("pe")
                pe_grp2 = em.val("pe")
                for tt in range(2):
                    ss = (dcout * 2 + tt) % 2
                    em.wait("dv", "pe", pe_grp2)
                    em.wait("dv", f"sg{ss}", dvstage[ss])
                    dv(lambda v, dcout=dcout, sd_c=sd_c: v.tensor_copy(
                        col["scc_c"][:, :], sd_c[:, dcout:dcout + 1]))
                    dv(lambda v, grp=grp, tt=tt, ss=ss: v.tensor_scalar(
                        stage[:, ss, :], P[2 * grp + tt][:, :],
                        col["scc_c"][:, :], None, ALU.mult))
                    em.wait("sy", "dv", em.val("dv"))
                    em.op("sy", lambda s, ss=ss: s.dma_start_transpose(
                        stageT[:, ss, :, :], stage[:, ss, :]), f"sg{ss}", 16)
                    dvstage[ss] = em.val(f"sg{ss}")
                    em.wait("dv", f"sg{ss}", em.val(f"sg{ss}"))
                    dv(lambda v, dcout=dcout, tt=tt, ss=ss: v.tensor_tensor(
                        resid[:, tt * 4:(tt + 1) * 4, dcout * 128:(dcout + 1) * 128],
                        resid[:, tt * 4:(tt + 1) * 4, dcout * 128:(dcout + 1) * 128],
                        stageT[:, ss, :, :], ALU.add))
                dv_add_done[grp] = em.val("dv")

        ffn(0)
        for t in range(NT):
            layer_norm(t, make_xfm=True)

        if PHASES <= 4:
            _finish()
            return nc
        WIBC = IMPB[:, 0:1024]
        WIROW = IMPB[0:1, 1024:2048]
        SCR = IMPB[:, 1024:2048]
        for half in range(2):
            em.wait("sy", "dv", em.val("dv"))
            em.op("sy", lambda s, half=half: s.dma_start(
                out=WIROW[:, :], in_=wi_d[0:1, half * 1024:(half + 1) * 1024]), "msy", 16)
            em.wait("te", "msy", em.val("msy"))
            em.wait("te", "dv", em.val("dv"))
            for dt in range(2):
                te(lambda t_, dt=dt: t_.matmul(
                    P[dt][:, :], ones_f[:, :], WIROW[:, dt * 512:(dt + 1) * 512],
                    start=True, stop=True))
            em.wait("dv", "pe", em.val("pe"))
            for dt in range(2):
                dv(lambda v, dt=dt: v.tensor_copy(
                    WIBC[:, dt * 512:(dt + 1) * 512], P[dt][:, :]))
            dst = impown if half == 0 else s16b
            for t in range(NT):
                dv(lambda v, t=t, half=half: v.tensor_tensor(
                    SCR[:, :], resid[:, t, half * 1024:(half + 1) * 1024],
                    WIBC[:, :], ALU.mult))
                dv(lambda v, t=t, dst=dst: v.reduce_sum(
                    dst[:, t:t + 1], SCR[:, :], axis=AX.X))
        dv(lambda v: v.tensor_tensor(impown[:, :], impown[:, :], s16b[:, 0:8], ALU.add))
        dv(lambda v: v.tensor_scalar(impown[:, :], impown[:, :], bic[:, :], None, ALU.add))
        em.wait("te", "dv", em.val("dv"))
        te(lambda t_: t_.transpose(P[6][0:8, 0:128], impown[:, :], identb[:, :]))
        em.wait("dv", "pe", em.val("pe"))
        dv(lambda v: v.tensor_copy(impT[:, :], P[6][0:8, 0:128]))
        em.wait("sy", "dv", em.val("dv"))
        em.op("sy", lambda s: s.dma_start(out=agi_in[:, :], in_=impT[:, :]), "imps", 16)
        em.wait("gp", "imps", em.val("imps"))
        em.op("gp", lambda g: g.collective_compute(
            "AllGather", ALU.bypass, replica_groups=RG,
            ins=[agi_in[:, :].opt()], outs=[agi_out[:, :].opt()]), "cc", 1)
        cc_agi = em.val("cc")
        em.wait("sy", "cc", cc_agi)
        em.wait("sy", "dv", em.val("dv"))
        for r16 in range(16):
            em.op("sy", lambda s, r16=r16: s.dma_start(
                out=IMPB[0:1, r16 * 128:(r16 + 1) * 128], in_=agi_out[r16:r16 + 1, :]),
                "imps", 16)
        em.wait("te", "imps", em.val("imps"))
        em.wait("te", "dv", em.val("dv"))
        for dt in range(4):
            te(lambda t_, dt=dt: t_.matmul(
                P[dt][:, :], ones_f[:, :], IMPB[0:1, dt * 512:(dt + 1) * 512],
                start=True, stop=True))
        em.wait("dv", "pe", em.val("pe"))
        for dt in range(4):
            dv(lambda v, dt=dt: v.tensor_copy(
                IMPB[:, dt * 512:(dt + 1) * 512], P[dt][:, :]))
        LO, HI, TMP, CNT, PRD, D1 = (s16[:, 0:1], s16[:, 1:2], s16[:, 2:3],
                                     s16[:, 3:4], s16[:, 4:5], s16[:, 5:6])
        dv(lambda v: v.tensor_reduce(LO, IMPB[:, :], AX.X, ALU.min))
        dv(lambda v: v.tensor_scalar(LO, LO, -1.0, None, ALU.add))
        dv(lambda v: v.reduce_max(HI, IMPB[:, :], axis=AX.X))
        dv(lambda v: v.tensor_scalar(HI, HI, 1.0, None, ALU.add))
        for it in range(N_ITER):
            dv(lambda v: v.tensor_tensor(TMP, LO, HI, ALU.add))
            dv(lambda v: v.tensor_scalar(col["mid_c"][:, :], TMP, 0.5, None, ALU.mult))
            dv(lambda v: v.tensor_scalar(GTB[:, :], IMPB[:, :],
                                         col["mid_c"][:, :], None, ALU.is_gt))
            dv(lambda v: v.reduce_sum(CNT, GTB[:, :], axis=AX.X))
            dv(lambda v: v.tensor_scalar(PRD, CNT, 409.5, None, ALU.is_gt))
            dv(lambda v: v.tensor_tensor(D1, col["mid_c"][:, :], LO, ALU.subtract))
            dv(lambda v: v.tensor_tensor(D1, PRD, D1, ALU.mult))
            dv(lambda v: v.tensor_tensor(LO, LO, D1, ALU.add))
            dv(lambda v: v.tensor_tensor(D1, HI, col["mid_c"][:, :], ALU.subtract))
            dv(lambda v: v.tensor_tensor(D1, PRD, D1, ALU.mult))
            dv(lambda v: v.tensor_tensor(HI, col["mid_c"][:, :], D1, ALU.add))
        dv(lambda v: v.tensor_copy(col["thr_c"][:, :], HI))
        for t in range(NT):
            em.wait("dv", f"xf{t}", em.val(f"xf{t}"))
        em.wait("sy", "dv", em.val("dv"))
        em.op("sy", lambda s: s.dma_start(
            out=SEL8[:, :].rearrange("p (a b) -> p a b", a=8), in_=sel8_d[:, :, :]),
            "msy", 16)
        em.wait("te", "msy", em.val("msy"))
        em.wait("te", "dv", em.val("dv"))
        for t in range(8):
            if t == 4:
                em.wait("te", "dv", em.val("dv"))
            te(lambda t_, t=t: t_.matmul(
                P[t % 4][:, 0:128], SEL8[:, t * 128:(t + 1) * 128], impT[:, :],
                start=True, stop=True))
            if t % 4 == 3 or t == 7:
                em.wait("dv", "pe", em.val("pe"))
                for t2 in range(t - t % 4, t + 1):
                    dv(lambda v, t2=t2: v.tensor_scalar(
                        MASK[:, t2 * 128:(t2 + 1) * 128], P[t2 % 4][:, 0:128],
                        col["thr_c"][:, :], None, ALU.is_gt))
        dv_mask = em.val("dv")
        if DEBUG_DUMP == 2:
            em.wait("sy", "dv", em.val("dv"))
            em.op("sy", lambda s: s.dma_start(out=dbg2_d[:, 32768:33792], in_=MASK[:, :]), "st", 16)
            em.op("sy", lambda s: s.dma_start(out=dbg3_d[:, 0:8], in_=impown[:, :]), "st", 16)
            em.op("sy", lambda s: s.dma_start(out=dbg3_d[:, 8:9], in_=col["thr_c"][:, :],
                                              allow_slow_non_contiguous=True), "st", 16)
            em.op("sy", lambda s: s.dma_start(out=dbg3_d[:, 16:22], in_=s16[:, 0:6]), "st", 16)

        for t in range(NT):
            em.wait("te", f"xf{t}", em.val(f"xf{t}"))
        pe_wslot = {i: em.val("pe") for i in range(3)}
        nsl = 0
        sgq_done = {0: em.val("sg0"), 1: em.val("sg1")}
        dv_qevac = {i: em.val("dv") for i in range(8)}
        for proj, (wdram, bcol, masked, agrow) in enumerate(
                [(l1q_d, bqc, False, None), (l1k_d, bkc, True, 0),
                 (l1v_d, bvc, True, 2048)]):
            for oc in range(16):
                slot = nsl % 3
                nsl += 1
                em.wait("gp", "pe", pe_wslot[slot])
                em.op("gp", lambda g, oc=oc, slot=slot, wdram=wdram: g.dma_start(
                    out=wsl[:, slot, :].rearrange("p (a b) -> p a b", a=16),
                    in_=wdram[oc, :, :, :]),
                    f"wl{slot}", 16)
                em.wait("te", f"wl{slot}", em.val(f"wl{slot}"))
                wvv = wsl[:, slot, :].rearrange("p (a b) -> p a b", a=16)
                grp2 = (proj * 16 + oc) % 4
                em.wait("te", "dv", dv_qevac[grp2 * 2 + 1])
                for tt in range(2):
                    pp = P[2 * grp2 + tt]
                    for dc in range(16):
                        te(lambda t_, pp=pp, wvv=wvv, dc=dc, tt=tt: t_.matmul(
                            pp[:, :], wvv[:, dc, :],
                            xfm[:, dc, tt * 512:(tt + 1) * 512],
                            start=(dc == 0), stop=(dc == 15)))
                pe_wslot[slot] = em.val("pe")
                em.wait("dv", "pe", em.val("pe"))
                dv(lambda v, bcol=bcol, oc=oc: v.tensor_copy(
                    col["bcol_c"][:, :], bcol[:, oc:oc + 1]))
                for tt in range(2):
                    if not masked:
                        dv(lambda v, oc=oc, tt=tt, grp2=grp2: v.tensor_scalar(
                            QFM[:, oc, tt * 512:(tt + 1) * 512],
                            P[2 * grp2 + tt][:, :], col["bcol_c"][:, :], None, ALU.add))
                    else:
                        ss = (oc * 2 + tt) % 2
                        em.wait("dv", f"sg{ss}", sgq_done[ss])
                        dv(lambda v, tt=tt, ss=ss, grp2=grp2: v.tensor_scalar(
                            stage[:, ss, :], P[2 * grp2 + tt][:, :],
                            col["bcol_c"][:, :], None, ALU.add))
                        dv(lambda v, tt=tt, ss=ss: v.tensor_tensor(
                            stage[:, ss, :], stage[:, ss, :],
                            MASK[:, tt * 512:(tt + 1) * 512], ALU.mult))
                        em.wait("sy", "dv", em.val("dv"))
                        em.op("sy", lambda s, oc=oc, tt=tt, ss=ss, agrow=agrow: s.dma_start(
                            out=agkv_in[agrow + oc * 128: agrow + (oc + 1) * 128,
                                        tt * 512:(tt + 1) * 512],
                            in_=stage[:, ss, :]), f"sg{ss}", 16)
                        sgq_done[ss] = em.val(f"sg{ss}")
                dv_qevac[grp2 * 2 + 1] = em.val("dv")
        em.wait("gp", "sg0", em.val("sg0"))
        em.wait("gp", "sg1", em.val("sg1"))
        em.op("gp", lambda g: g.collective_compute(
            "AllGather", ALU.bypass, replica_groups=RG,
            ins=[agkv_in[:, :].opt()], outs=[agkv_out[:, :].opt()]), "cc", 1)
        cc_agkv = em.val("cc")

        em.wait("sy", "cc", cc_agkv)
        sc_exp_psum = {0: em.val("sc"), 1: em.val("sc")}
        pe_slotE = [em.val("pe")] * 4
        dv_out_evac = em.val("dv")
        pe_kh = {0: em.val("pe"), 1: em.val("pe")}
        pe_prev_head = em.val("pe")
        vt_prev = em.val("vt")
        ecnt = 0
        for h in range(16):
            hs = h % 2
            em.wait("sy", "pe", pe_kh[hs])
            em.op("sy", lambda s, h=h, hs=hs: s.dma_start(
                out=KH[hs][:, 0:1024], in_=agkv_out[h * 128:(h + 1) * 128, :]),
                f"kh{hs}", 16)
            em.op("sy", lambda s, h=h, hs=hs: s.dma_start(
                out=KH[hs][:, 1024:2048],
                in_=agkv_out[4096 + h * 128: 4096 + (h + 1) * 128, :]), f"kh{hs}", 16)
            em.wait("sy", "vt", vt_prev)
            em.wait("sy", "pe", pe_prev_head)
            em.op("sy", lambda s, h=h: s.dma_start(
                out=VFMh[:, 0:1024], in_=agkv_out[2048 + h * 128: 2048 + (h + 1) * 128, :]),
                f"kh{hs}", 16)
            em.op("sy", lambda s, h=h: s.dma_start(
                out=VFMh[:, 1024:2048],
                in_=agkv_out[6144 + h * 128: 6144 + (h + 1) * 128, :]), f"kh{hs}", 16)
            kh_v = em.val(f"kh{hs}")
            em.wait("sy", f"kh{hs}", kh_v)
            em.wait("sy", "pe", pe_kh[hs])
            em.op("sy", lambda s, hs=hs: s.dma_start_transpose(
                VTMh[:, :, :], VFMh[:, :]), "vt", 16)
            vt_prev = em.val("vt")
            em.wait("te", f"kh{hs}", kh_v)
            em.wait("te", "vt", vt_prev)
            khv = KH[hs].rearrange("p (a b) -> p a b", a=16)
            for qt in range(2):
                for jc in range(16):
                    sp = jc % 2
                    em.wait("te", "sc", sc_exp_psum[sp])
                    te(lambda t_, khv=khv, jc=jc, sp=sp, h=h, qt=qt: t_.matmul(
                        P[sp][:, :], khv[:, jc, :],
                        QFM[:, h, qt * 512:(qt + 1) * 512], start=True, stop=True))
                    pe_sc = em.val("pe")
                    esl = ecnt % 4
                    ecnt += 1
                    em.wait("sc", "pe", pe_sc)
                    em.wait("sc", "pe", pe_slotE[esl])
                    sc(lambda s_, sp=sp, esl=esl: s_.activation(
                        out=EB[:, esl, :], in_=P[sp][:, :], func=AF.Exp,
                        bias=col["c_n8"][:, :], scale=float(HD ** -0.5)))
                    sc_exp_psum[sp] = em.val("sc")
                    em.wait("te", "sc", em.val("sc"))
                    if jc == 0:
                        em.wait("te", "dv", dv_out_evac)
                    te(lambda t_, esl=esl, jc=jc: t_.matmul(
                        P[4][:, :], ones_bf[:, :], EB[:, esl, :],
                        start=(jc == 0), stop=(jc == 15)))
                    te(lambda t_, esl=esl, jc=jc: t_.matmul(
                        P[5][:, :], VTMh[:, jc, :], EB[:, esl, :],
                        start=(jc == 0), stop=(jc == 15)))
                    pe_slotE[esl] = em.val("pe")
                em.wait("dv", "pe", em.val("pe"))
                dv(lambda v: v.reciprocal(RSCRc[:, :], P[4][:, :]))
                dv(lambda v, h=h, qt=qt: v.tensor_tensor(
                    OUT1[:, h, qt * 512:(qt + 1) * 512], P[5][:, :],
                    RSCRc[:, :], ALU.mult))
                dv_out_evac = em.val("dv")
            pe_kh[hs] = em.val("pe")
            pe_prev_head = em.val("pe")

        if DEBUG_DUMP == 2:
            em.wait("sy", "dv", em.val("dv"))
            em.wait("sy", "pe", em.val("pe"))
            em.op("sy", lambda s: s.dma_start(
                out=dbg2_d[:, 0:16384], in_=QFM[:, :, :].rearrange("p a b -> p (a b)")), "st", 16)
            em.op("sy", lambda s: s.dma_start(
                out=dbg2_d[:, 16384:32768], in_=OUT1[:, :, :].rearrange("p a b -> p (a b)")), "st", 16)
        pe_wslot = {i: em.val("pe") for i in range(3)}
        nsl = 0
        dvstage = {0: em.val("sg0"), 1: em.val("sg1")}
        dv_add_done = {0: em.val("dv"), 1: em.val("dv")}
        for dcout in range(16):
            grp = dcout % 2
            slot = nsl % 3
            nsl += 1
            em.wait("gp", "pe", pe_wslot[slot])
            em.op("gp", lambda g, dcout=dcout, slot=slot: g.dma_start(
                out=wsl[:, slot, :].rearrange("p (a b) -> p a b", a=16),
                in_=l1o_d[dcout, :, :, :]),
                f"wl{slot}", 16)
            em.wait("te", f"wl{slot}", em.val(f"wl{slot}"))
            em.wait("te", "dv", dv_add_done[grp])
            wvv = wsl[:, slot, :].rearrange("p (a b) -> p a b", a=16)
            for tt in range(2):
                pp = P[2 * grp + tt]
                for oc in range(16):
                    te(lambda t_, pp=pp, wvv=wvv, oc=oc, tt=tt: t_.matmul(
                        pp[:, :], wvv[:, oc, :],
                        OUT1[:, oc, tt * 512:(tt + 1) * 512],
                        start=(oc == 0), stop=(oc == 15)))
            pe_wslot[slot] = em.val("pe")
            pe_grp2 = em.val("pe")
            for tt in range(2):
                ss = (dcout * 2 + tt) % 2
                em.wait("dv", "pe", pe_grp2)
                em.wait("dv", f"sg{ss}", dvstage[ss])
                dv(lambda v, dcout=dcout: v.tensor_copy(
                    col["bcol_c"][:, :], boc[:, dcout:dcout + 1]))
                dv(lambda v, grp=grp, tt=tt, ss=ss: v.tensor_scalar(
                    stage[:, ss, :], P[2 * grp + tt][:, :],
                    col["bcol_c"][:, :], None, ALU.add))
                em.wait("sy", "dv", em.val("dv"))
                em.op("sy", lambda s, ss=ss: s.dma_start_transpose(
                    stageT[:, ss, :, :], stage[:, ss, :]), f"sg{ss}", 16)
                dvstage[ss] = em.val(f"sg{ss}")
                em.wait("dv", f"sg{ss}", em.val(f"sg{ss}"))
                dv(lambda v, dcout=dcout, tt=tt, ss=ss: v.tensor_tensor(
                    resid[:, tt * 4:(tt + 1) * 4, dcout * 128:(dcout + 1) * 128],
                    resid[:, tt * 4:(tt + 1) * 4, dcout * 128:(dcout + 1) * 128],
                    stageT[:, ss, :, :], ALU.add))
            dv_add_done[grp] = em.val("dv")
        for t in range(NT):
            layer_norm(t, make_xfm=True)

        if PHASES <= 5:
            _finish()
            return nc
        ffn(1)
        for t in range(NT):
            layer_norm(t, make_xfm=False)

        em.wait("sy", "dv", em.val("dv"))
        for t in range(NT):
            em.op("sy", lambda s, t=t: s.dma_start(
                out=out_d[t * 128:(t + 1) * 128, :], in_=resid[:, t, :]), "st", 16)
        em.wait("sy", "st", em.val("st"))

        _replay()

    return nc


def kernel(**inputs):
    maps = _host_prepare(inputs)
    nc = build_nc()
    res = run_bass_kernel_spmd(nc, maps, core_ids=list(range(8)))
    out = np.zeros((B, S, D), np.float32)
    for c in range(8):
        out[c // 2, (c % 2) * 1024:(c % 2 + 1) * 1024] = res.results[c]["out_own"]
    return out


if __name__ == "__main__":
    nc = build_nc()
    print("build ok")
